# version 02
# baseline (speedup 1.0000x reference)
import numpy as np
import ml_dtypes
from contextlib import ExitStack

import concourse.bass as bass
import concourse.mybir as mybir
from concourse.bass_utils import run_bass_kernel_spmd
from concourse.bass_core_rust import IndirectOffsetOnAxis

F32 = mybir.dt.float32
BF16 = mybir.dt.bfloat16
I32 = mybir.dt.int32
AF = mybir.ActivationFunctionType
ALU = mybir.AluOpType
AXX = mybir.AxisListType.X
BFNP = ml_dtypes.bfloat16

B, T, D, H, DH, NE, HID = 2, 8192, 1024, 16, 64, 8, 4096
CAP = 2048
NCORE = 8
SL = 2048
NT = SL // 128
NF = D // 128
EPS_LN = 1e-5
EPS_Z = 1e-6
BIGIDX = float(1 << 22)


def consts_np():
    ident = np.eye(128, dtype=BFNP)
    maskT = np.triu(np.ones((128, 128), np.float32)).astype(BFNP)
    blockpat = np.kron(np.eye(H, dtype=np.float32),
                       np.ones((DH, 1), np.float32)).astype(BFNP)
    E = np.zeros((H, D), np.float32)
    for m in range(NF):
        for p in range(128):
            E[2 * m + p // 64, 128 * m + p] = 1.0
    return ident, maskT, blockpat, E.astype(BFNP)


class Sem:
    def __init__(self, C, name):
        self.h = C.es.enter_context(C.nc.semaphore(name))
        self.v = 0

    def inc(self, instr, k=1):
        instr.then_inc(self.h, k)
        self.v += k
        return self.v


class Ctx:
    def __init__(self, nc, es):
        self.nc, self.es = nc, es

    def sb(self, name, shape, dtype):
        return self.es.enter_context(self.nc.sbuf_tensor(name, shape, dtype)).ap()

    def ps(self, name, shape, dtype=F32):
        return self.es.enter_context(self.nc.psum_tensor(name, shape, dtype)).ap()

    def sem(self, name):
        return Sem(self, name)



class Chain:

    def __init__(self, C, name):
        self.s = C.sem(name)

    def bar(self, eng, ins):
        self.s.inc(ins)
        eng.wait_ge(self.s.h, self.s.v)


def emit_L1(nc, xs, wq_d, wk_d, wv_d, qz_o, kf_o, ktm_o, vtm_o, ss_o,
            ident_c, bp_c, E_c, pref=""):
    with ExitStack() as es:
        C = Ctx(nc, es)
        _sb0, _sem0 = C.sb, C.sem
        C.sb = lambda n, sh, dt: _sb0(pref + n, sh, dt)
        C.ps = (lambda f: (lambda n, sh, dt=F32: f(pref + n, sh, dt)))(C.ps)
        C.sem = lambda n: _sem0(pref + n)
        wbuf = [C.sb(f"w{i}", [128, NF * 1024], BF16) for i in range(2)]
        xh = C.sb("xh", [128, 4 * 1024], F32)
        xn_tm = C.sb("xn_tm", [128, NT * 1024], BF16)
        scrA = C.sb("scrA", [128, 4096], F32)
        scrA_bf = scrA.tensor.bitcast(BF16).ap()
        fa = scrA[:, 0:512]
        fe = scrA[:, 512:1024]
        zz = scrA[0:16, 0:2048]
        zbb = scrA_bf[0:16, 4096:6144]
        stat = C.sb("stat", [128, 64], F32)
        sq4 = scrA
        cns = C.sb("cns", [128, 2], F32)
        fscr = C.sb("fscr", [128, 2], F32)
        xn_fm = [C.sb(f"xnf{k}", [128, SL], BF16) for k in range(NF)]
        qf = [C.sb(f"qf{k}", [128, SL], BF16) for k in range(NF)]
        kf = [C.sb(f"kf{k}", [128, SL], BF16) for k in range(NF)]
        vrot = [C.sb(f"vr{i}", [128, D], BF16) for i in range(2)]
        ktm_rot = [C.sb(f"ktr{i}", [128, D], BF16) for i in range(2)]
        ident = C.sb("identS", [128, 128], BF16)
        bp = C.sb("bpS", [128, 16 * NF], BF16)
        Ehb = C.sb("EhbS", [16, D], BF16)
        ssum = C.sb("ssumS", [128, 512], F32)

        ps_mm = [C.ps(f"pmm{i}", [128, 512], F32) for i in range(4)]
        ps_tp = [C.ps(f"ptp{i}", [128, 1024], BF16) for i in range(2)]
        ps_z = C.ps("psz", [16, 512], F32)

        s_c = C.sem("consts")
        s_xh = [C.sem("xh0"), C.sem("xh1")]
        s_w = C.sem("w")
        s_cns = C.sem("cnsS")
        s_st = C.sem("stS")
        s_sd = C.sem("sdS")
        s_xn = C.sem("xnS")
        s_tp = C.sem("tpS")
        s_te = C.sem("teS")
        chv = Chain(C, "chvA")

        with nc.Block() as blk:
            @blk.sync
            def _(sy):
                sy.dma_start(out=ident, in_=ident_c).then_inc(s_c.h, 16)
                for k in range(NF):
                    sy.dma_start(out=bp[:, 16 * k:16 * (k + 1)],
                                 in_=bp_c[128 * k:128 * (k + 1), :]
                                 ).then_inc(s_c.h, 16)
                sy.dma_start(out=Ehb, in_=E_c).then_inc(s_c.h, 16)
                s_c.v += 160
                for hf in range(4):
                    if hf >= 1:
                        sy.wait_ge(s_xn.h, hf)
                    for j in range(4):
                        r0 = 512 * hf + 128 * j
                        sy.dma_start(out=xh[:, 1024 * j:1024 * (j + 1)],
                                     in_=xs[r0:r0 + 128, :]
                                     ).then_inc(s_xh[hf % 2].h, 16)
                        s_xh[hf % 2].v += 16

            @blk.gpsimd
            def _(g):
                g.dma_start(out=wbuf[0].rearrange("p (k m) -> p k m", m=1024),
                            in_=wq_d.rearrange("(k p) m -> p k m", p=128)
                            ).then_inc(s_w.h, 16)
                g.dma_start(out=wbuf[1].rearrange("p (k m) -> p k m", m=1024),
                            in_=wk_d.rearrange("(k p) m -> p k m", p=128)
                            ).then_inc(s_w.h, 16)
                s_w.v += 32
                g.memset(cns[:, 0:1], EPS_LN)
                g.memset(cns[:, 1:2], 2.0 ** -30).then_inc(s_cns.h, 1)
                s_cns.v += 1
                g.wait_ge(s_w.h, 32)

            @blk.vector
            def _(v):
                for hf in range(4):
                    c0 = 4 * hf
                    v.wait_ge(s_xh[hf % 2].h, 64 * (hf // 2 + 1))
                    for j in range(4):
                        ins = v.reduce_sum(stat[:, c0 + j:c0 + j + 1],
                                           xh[:, 1024 * j:1024 * (j + 1)],
                                           axis=AXX)
                    chv.bar(v, ins)
                    for j in range(4):
                        ins = v.tensor_scalar(
                            xh[:, 1024 * j:1024 * (j + 1)],
                            xh[:, 1024 * j:1024 * (j + 1)],
                            1024.0, stat[:, c0 + j:c0 + j + 1],
                            ALU.mult, ALU.subtract)
                    chv.bar(v, ins)
                    for j in range(4):
                        ins = v.tensor_mul(sq4[:, 1024 * j:1024 * (j + 1)],
                                           xh[:, 1024 * j:1024 * (j + 1)],
                                           xh[:, 1024 * j:1024 * (j + 1)])
                    chv.bar(v, ins)
                    for j in range(4):
                        i = c0 + j
                        ins = v.reduce_sum(stat[:, 16 + i:16 + i + 1],
                                           sq4[:, 1024 * j:1024 * (j + 1)],
                                           axis=AXX)
                    s_st.inc(ins)
                    v.wait_ge(s_sd.h, hf + 1)
                    ins = v.reciprocal(stat[:, 48 + c0:48 + c0 + 4],
                                       stat[:, 32 + c0:32 + c0 + 4])
                    chv.bar(v, ins)
                    for j in range(4):
                        i = c0 + j
                        ins = v.tensor_scalar(
                            xn_tm[:, 1024 * i:1024 * (i + 1)],
                            xh[:, 1024 * j:1024 * (j + 1)],
                            stat[:, 48 + i:48 + i + 1], 1.0 / 1024.0,
                            ALU.mult, ALU.mult)
                    s_xn.inc(ins)

            @blk.scalar
            def _(s):
                for hf in range(4):
                    c0 = 4 * hf
                    s.wait_ge(s_st.h, hf + 1)
                    if hf == 0:
                        s.wait_ge(s_cns.h, 1)
                    s_sd.inc(s.activation(out=stat[:, 32 + c0:32 + c0 + 4],
                                          in_=stat[:, 16 + c0:16 + c0 + 4],
                                          func=AF.Sqrt, bias=cns[:, 0:1],
                                          scale=cns[:, 1:2]))
                for i in range(NT):
                    s.wait_ge(s_tp.h, i + 1)
                    for k in range(NF):
                        ins = s.activation(
                            out=xn_fm[k][:, 128 * i:128 * (i + 1)],
                            in_=ps_tp[i % 2][:, 128 * k:128 * (k + 1)],
                            func=AF.Copy)
                    s_te.inc(ins)

            @blk.tensor
            def _(t):
                t.wait_ge(s_c.h, 160)
                for i in range(NT):
                    t.wait_ge(s_xn.h, i // 4 + 1)
                    if i >= 2:
                        t.wait_ge(s_te.h, i - 1)
                    for k in range(NF):
                        ins = t.transpose(
                            ps_tp[i % 2][:, 128 * k:128 * (k + 1)],
                            xn_tm[:, 1024 * i + 128 * k:1024 * i + 128 * (k + 1)],
                            ident)
                    s_tp.inc(ins)
        nc.all_engine_barrier()

        s_pe = C.sem("peB")
        s_ev = C.sem("evB")
        s_ev2 = C.sem("evB2")
        s_fa = C.sem("faB")
        s_fe = C.sem("feB")
        s_w2 = C.sem("wB")
        s_vo = [C.sem("vo0"), C.sem("vo1")]

        groups_q = [(m, n) for m in range(NF) for n in range(SL // 512)]
        NG = len(groups_q)

        with nc.Block() as blk:
            @blk.gpsimd
            def _(g):
                g.wait_ge(s_pe.h, NG)
                g.dma_start(out=wbuf[0].rearrange("p (k m) -> p k m", m=1024),
                            in_=wv_d.rearrange("(k p) m -> p k m", p=128)
                            ).then_inc(s_w2.h, 16)
                s_w2.v += 16
                for mt in range(NT):
                    g.wait_ge(s_ev2.h, 2 * (mt + 1))
                    g.dma_start(out=vtm_o[128 * mt:128 * (mt + 1), :],
                                in_=vrot[mt % 2]).then_inc(s_vo[mt % 2].h, 16)
                    s_vo[mt % 2].v += 16
                g.wait_ge(s_vo[0].h, s_vo[0].v)
                g.wait_ge(s_vo[1].h, s_vo[1].v)

            @blk.tensor
            def _(t):
                gi = 0
                for src in (0, 1):
                    for (m, n) in groups_q:
                        if gi >= 4:
                            t.wait_ge(s_ev.h, gi - 3)
                        for k in range(NF):
                            ins = t.matmul(
                                ps_mm[gi % 4],
                                wbuf[src][:, 1024 * k + 128 * m:
                                          1024 * k + 128 * (m + 1)],
                                xn_fm[k][:, 512 * n:512 * (n + 1)],
                                start=(k == 0), stop=(k == NF - 1))
                        s_pe.inc(ins)
                        gi += 1
                t.wait_ge(s_w2.h, 16)
                for mt in range(NT):
                    for n in range(2):
                        if gi - 4 < 2 * NG:
                            t.wait_ge(s_ev.h, gi - 3)
                        else:
                            t.wait_ge(s_ev2.h, gi - 3 - 2 * NG)
                        for k in range(NF):
                            ins = t.matmul(
                                ps_mm[gi % 4],
                                xn_fm[k][:, 128 * mt:128 * (mt + 1)],
                                wbuf[0][:, 1024 * k + 512 * n:
                                        1024 * k + 512 * (n + 1)],
                                start=(k == 0), stop=(k == NF - 1))
                        s_pe.inc(ins)
                        gi += 1

            @blk.vector
            def _(v):
                gi = 0
                for dstbuf in (qf, kf):
                    for (m, n) in groups_q:
                        v.wait_ge(s_pe.h, gi + 1)
                        s_fa.inc(v.tensor_scalar_min(fa, ps_mm[gi % 4], 0.0))
                        v.wait_ge(s_fe.h, gi + 1)
                        s_ev.inc(v.scalar_tensor_tensor(
                            dstbuf[m][:, 512 * n:512 * (n + 1)],
                            ps_mm[gi % 4], 0.0, fe, ALU.max, ALU.add))
                        gi += 1

            @blk.scalar
            def _(s):
                for gi in range(2 * NG):
                    s.wait_ge(s_fa.h, gi + 1)
                    s_fe.inc(s.activation(out=fe, in_=fa, func=AF.Exp))
                vg = 2 * NG
                for mt in range(NT):
                    for n in range(2):
                        s.wait_ge(s_pe.h, vg + 1)
                        if mt >= 2 and n == 0:
                            s.wait_ge(s_vo[mt % 2].h, 16 * (mt // 2))
                        s_ev2.inc(s.activation(
                            out=vrot[mt % 2][:, 512 * n:512 * (n + 1)],
                            in_=ps_mm[vg % 4], func=AF.Copy))
                        vg += 1
        nc.all_engine_barrier()

        s_tp2 = C.sem("tpC")
        s_te2 = C.sem("teC")
        s_sc = C.sem("scC")
        s_sa = C.sem("saC")
        s_go = [C.sem("go0"), C.sem("go1")]
        s_do = C.sem("doC")
        s_sf = C.sem("sfC")
        s_vl = [C.sem("vl0"), C.sem("vl1")]
        chc = Chain(C, "chvC")

        with nc.Block() as blk:
            @blk.scalar
            def _(s):
                for k in range(NF):
                    s.dma_start(out=kf_o[128 * k:128 * (k + 1), :],
                                in_=kf[k]).then_inc(s_do.h, 16)
                    s_do.v += 16
                for c in range(NT):
                    s.wait_ge(s_tp2.h, c + 1)
                    if c >= 2:
                        s.wait_ge(s_sc.h, c - 1)
                        s.wait_ge(s_go[c % 2].h, 16 * (c // 2))
                    s_te2.inc(s.activation(out=ktm_rot[c % 2], in_=ps_tp[c % 2],
                                           func=AF.Copy))
                s.wait_ge(s_do.h, s_do.v)

            @blk.sync
            def _(sy):
                for c in range(NT):
                    if c >= 2:
                        sy.wait_ge(s_sc.h, c - 1)
                    sy.dma_start(out=vrot[c % 2],
                                 in_=vtm_o[128 * c:128 * (c + 1), :]
                                 ).then_inc(s_vl[c % 2].h, 16)
                    s_vl[c % 2].v += 16

            @blk.tensor
            def _(t):
                for c in range(NT):
                    if c >= 2:
                        t.wait_ge(s_te2.h, c - 1)
                    for k in range(NF):
                        ins = t.transpose(ps_tp[c % 2][:, 128 * k:128 * (k + 1)],
                                          kf[k][:, 128 * c:128 * (c + 1)], ident)
                    s_tp2.inc(ins)
                    t.wait_ge(s_te2.h, c + 1)
                    t.wait_ge(s_vl[c % 2].h, 16 * (c // 2 + 1))
                    if c >= 2:
                        t.wait_ge(s_sa.h, c - 1)
                    for h in range(H):
                        ins = t.matmul(
                            ps_mm[c % 2][64 * (h % 2):64 * (h % 2) + 64,
                                         64 * (h // 2):64 * (h // 2) + 64],
                            ktm_rot[c % 2][:, 64 * h:64 * (h + 1)],
                            vrot[c % 2][:, 64 * h:64 * (h + 1)],
                            start=True, stop=True, skip_group_check=True)
                    s_sc.inc(ins)

            @blk.gpsimd
            def _(g):
                for c in range(NT):
                    g.wait_ge(s_te2.h, c + 1)
                    g.dma_start(out=ktm_o[128 * c:128 * (c + 1), :],
                                in_=ktm_rot[c % 2]).then_inc(s_go[c % 2].h, 16)
                    s_go[c % 2].v += 16
                g.wait_ge(s_go[0].h, s_go[0].v)
                g.wait_ge(s_go[1].h, s_go[1].v)

            @blk.vector
            def _(v):
                for c in range(NT):
                    v.wait_ge(s_sc.h, c + 1)
                    if c == 0:
                        ins = v.tensor_copy(out=ssum, in_=ps_mm[0])
                    else:
                        ins = v.tensor_add(ssum, ssum, ps_mm[c % 2])
                    s_sa.inc(ins)
                    v.wait_ge(s_sa.h, s_sa.v)
                s_sf.inc(v.tensor_copy(out=fscr[:, 0:1], in_=ssum[:, 511:512]))

            @blk.scalar
            def _(s):
                s.wait_ge(s_sf.h, 1)
                s.dma_start(out=ss_o, in_=ssum).then_inc(s_do.h, 16)
                s_do.v += 16
                s.wait_ge(s_do.h, s_do.v)
        nc.all_engine_barrier()

        s_d1 = C.sem("d1")
        s_d2 = C.sem("d2")
        s_d3 = C.sem("d3")
        s_d4 = C.sem("d4")
        s_d5 = C.sem("d5")
        s_d6 = C.sem("d6")
        s_do2 = C.sem("do2")
        chd = Chain(C, "chvD")
        NZG = SL // 512

        with nc.Block() as blk:
            @blk.vector
            def _(v):
                cp0 = scrA_bf[0:64, 0:2048]
                cps = [scrA_bf[0:64, 2048:4096], scrA_bf[0:64, 4096:6144]]
                for m in range(NF):
                    lo = kf[m][0:64, :]
                    hi = kf[m][64:128, :]
                    chd.bar(v, v.tensor_copy(out=cp0, in_=hi))
                    if m > 0:
                        chd.bar(v, v.tensor_add(lo, lo, cps[(m - 1) % 2]))
                    chd.bar(v, v.tensor_add(cps[m % 2], cp0, lo))
                    chd.bar(v, v.tensor_copy(out=hi, in_=cps[m % 2]))
                for m in range(NF):
                    ins = v.tensor_mul(xn_fm[m], qf[m], kf[m])
                s_d1.inc(ins)
                for n in range(NZG):
                    v.wait_ge(s_d2.h, n + 1)
                    ins = v.tensor_scalar_add(zz[:, 512 * n:512 * (n + 1)],
                                              ps_z, EPS_Z)
                    s_d3.inc(ins)
                v.wait_ge(s_d3.h, s_d3.v)
                chd.bar(v, v.reciprocal(zz, zz))
                ins = v.tensor_copy(zbb, zz)
                s_d3.inc(ins)
                gi = 0
                for m in range(NF):
                    for n in range(NZG):
                        v.wait_ge(s_d4.h, gi + 1)
                        ins = v.tensor_mul(qf[m][:, 512 * n:512 * (n + 1)],
                                           qf[m][:, 512 * n:512 * (n + 1)],
                                           ps_mm[gi % 4])
                        s_d5.inc(ins)
                        gi += 1
                    v.wait_ge(s_d5.h, gi)
                    s_d6.inc(v.tensor_copy(out=fscr[:, 1:2],
                                           in_=qf[m].bitcast(F32)[:, 1023:1024]))

            @blk.tensor
            def _(t):
                t.wait_ge(s_d1.h, 1)
                for n in range(NZG):
                    if n > 0:
                        t.wait_ge(s_d3.h, n)
                    for k in range(NF):
                        ins = t.matmul(ps_z, bp[:, 16 * k:16 * (k + 1)],
                                       xn_fm[k][:, 512 * n:512 * (n + 1)],
                                       start=(k == 0), stop=(k == NF - 1))
                    s_d2.inc(ins)
                t.wait_ge(s_d3.h, NZG + 1)
                gi = 0
                for m in range(NF):
                    for n in range(NZG):
                        if gi >= 4:
                            t.wait_ge(s_d5.h, gi - 3)
                        s_d4.inc(t.matmul(ps_mm[gi % 4],
                                          Ehb[:, 128 * m:128 * (m + 1)],
                                          zbb[:, 512 * n:512 * (n + 1)],
                                          start=True, stop=True))
                        gi += 1

            @blk.gpsimd
            def _(g):
                for m in range(NF):
                    g.wait_ge(s_d6.h, m + 1)
                    g.dma_start(out=qz_o[128 * m:128 * (m + 1), :],
                                in_=qf[m]).then_inc(s_do2.h, 16)
                    s_do2.v += 16
                g.wait_ge(s_do2.h, s_do2.v)





def build_L1():
    nc = bass.Bass("TRN2", target_bir_lowering=False, debug=False)
    xs = nc.dram_tensor("xs", [SL, D], F32, kind="ExternalInput").ap()
    wq_d = nc.dram_tensor("wq", [D, D], F32, kind="ExternalInput").ap()
    wk_d = nc.dram_tensor("wk", [D, D], F32, kind="ExternalInput").ap()
    wv_d = nc.dram_tensor("wv", [D, D], F32, kind="ExternalInput").ap()
    qz_o = nc.dram_tensor("qz", [D, SL], BF16, kind="ExternalOutput").ap()
    kf_o = nc.dram_tensor("kf", [D, SL], BF16, kind="ExternalOutput").ap()
    ktm_o = nc.dram_tensor("ktm", [SL, D], BF16, kind="ExternalOutput").ap()
    vtm_o = nc.dram_tensor("vtm", [SL, D], BF16, kind="ExternalOutput").ap()
    ss_o = nc.dram_tensor("ssum", [128, 512], F32, kind="ExternalOutput").ap()
    ident_np, _, blockpat_np, E_np = consts_np()
    ident_c = nc.inline_tensor(ident_np, name="identC").ap()
    bp_c = nc.inline_tensor(blockpat_np, name="bpC").ap()
    E_c = nc.inline_tensor(E_np, name="EC").ap()
    emit_L1(nc, xs, wq_d, wk_d, wv_d, qz_o, kf_o, ktm_o, vtm_o, ss_o,
            ident_c, bp_c, E_c)
    return nc


def np_feature(x):
    return np.where(x > 0, x + 1.0, np.exp(x))


def np_ln(x):
    m = x.mean(-1, keepdims=True)
    v = x.var(-1, keepdims=True)
    return (x - m) / np.sqrt(v + EPS_LN)





def emit_L2(nc, xs, qz_d, kf_d, ktm_d, vtm_d, sall_d, msk_d, wo_d, bo_d,
            gw_d, r1_o, xn2_o, probs_o, ident_c, maskT_c, ones_c, ones8f_c,
            ones18f_c, NSRC=NCORE, pref="", nph=4, sparts=7):
    with ExitStack() as es:
        C = Ctx(nc, es)
        _sb0, _sem0 = C.sb, C.sem
        C.sb = lambda n, sh, dt: _sb0(pref + n, sh, dt)
        C.ps = (lambda f: (lambda n, sh, dt=F32: f(pref + n, sh, dt)))(C.ps)
        C.sem = lambda n: _sem0(pref + n)
        kfs = [C.sb(f"kfs{i}", [128, 1024], BF16) for i in range(2)]
        qzs = [C.sb(f"qzs{i}", [128, 1024], BF16) for i in range(2)]
        kts = [C.sb(f"kts{i}", [128, 1024], BF16) for i in range(2)]
        vts = [C.sb(f"vts{i}", [128, 1024], BF16) for i in range(2)]
        attb = C.sb("attb", [128, 2048], BF16)
        sbf = [C.sb(f"sbf{i}", [128, 512], BF16) for i in range(2)]
        Sf32 = C.sb("Sf32", [128, 512], F32)
        sal = C.sb("sal", [128, NSRC * 512], F32)
        msk = C.sb("mskS", [128, NSRC], F32)
        stream = [C.sb(f"st{g}", [128, SL], BF16) for g in range(NF)]
        wo_b = C.sb("wo_b", [128, NF * 1024], BF16)
        bo_b = C.sb("bo_b", [1, D], BF16)
        gw_b = C.sb("gw_b", [128, NF * NE], BF16)
        onesr = C.sb("onesr", [1, 128], BF16)
        ones8 = C.sb("ones8", [8, 1], F32)
        ones18 = C.sb("ones18", [1, 8], F32)
        ident = C.sb("identS", [128, 128], BF16)
        maskT = C.sb("maskTS", [128, 128], BF16)
        xq = C.sb("xq", [128, 4096], F32)
        xcq = C.sb("xcq", [128, 4096], F32)
        xn2q = C.sb("xn2q", [128, 4096], BF16)
        xn2_fm = [C.sb(f"x2f{k}", [128, SL], BF16) for k in range(NF)]
        stat = C.sb("stat", [128, 64], F32)
        cns = C.sb("cns", [128, 2], F32)
        esc = C.sb("esc", [16, SL], F32)
        rec = C.sb("rec", [1, SL], F32)
        prb = C.sb("prb", [16, SL], F32)

        s_c = C.sem("consts")
        s_w = C.sem("w")
        s_cns = C.sem("cnsS")
        s_ini = C.sem("ini")
        s_sb = C.sem("sbS")

        with nc.Block() as blk:
            @blk.sync
            def _(sy):
                sy.dma_start(out=ident, in_=ident_c).then_inc(s_c.h, 16)
                sy.dma_start(out=maskT, in_=maskT_c).then_inc(s_c.h, 16)
                sy.dma_start(out=onesr, in_=ones_c).then_inc(s_c.h, 16)
                sy.dma_start(out=ones8, in_=ones8f_c).then_inc(s_c.h, 16)
                sy.dma_start(out=ones18, in_=ones18f_c).then_inc(s_c.h, 16)
                sy.dma_start(out=msk, in_=msk_d).then_inc(s_c.h, 16)
                sy.dma_start(out=sal.rearrange("p (c s) -> p c s", s=512),
                             in_=sall_d.rearrange("c p s -> p c s")
                             ).then_inc(s_c.h, 16)
                s_c.v += 112

            @blk.gpsimd
            def _(g):
                g.dma_start(out=wo_b.rearrange("p (k m) -> p k m", m=1024),
                            in_=wo_d.rearrange("(k p) m -> p k m", p=128)
                            ).then_inc(s_w.h, 16)
                g.dma_start(out=bo_b, in_=bo_d).then_inc(s_w.h, 16)
                g.dma_start(out=gw_b.rearrange("p (k e) -> p k e", e=NE),
                            in_=gw_d.rearrange("(k p) e -> p k e", p=128)
                            ).then_inc(s_w.h, 16)
                s_w.v += 48
                g.memset(cns[:, 0:1], EPS_LN)
                g.memset(cns[:, 1:2], 2.0 ** -30).then_inc(s_cns.h, 1)
                s_cns.v += 1
                g.wait_ge(s_w.h, s_w.v)

            @blk.vector
            def _(v):
                v.wait_ge(s_c.h, 112)
                ins = v.tensor_scalar(Sf32, sal[:, 0:512], msk[:, 0:1], None,
                                      ALU.mult)
                s_ini.inc(ins)
                v.wait_ge(s_ini.h, s_ini.v)
                for cc in range(1, NSRC):
                    ins = v.scalar_tensor_tensor(
                        Sf32, sal[:, 512 * cc:512 * (cc + 1)],
                        msk[:, cc:cc + 1], Sf32, ALU.mult, ALU.add)
                    s_ini.inc(ins)
                    v.wait_ge(s_ini.h, s_ini.v)
                ins = v.tensor_copy(out=sbf[0], in_=Sf32)
                s_sb.inc(ins)
        nc.all_engine_barrier()

        if nph < 2:
            return
        s_kf = [C.sem("kf0"), C.sem("kf1")]
        s_qz = [C.sem("qzS0"), C.sem("qzS1")]
        s_kt = [C.sem("kt0"), C.sem("kt1")]
        s_vt = [C.sem("vt0"), C.sem("vt1")]
        s_a = C.sem("aS")
        s_m = C.sem("mS")
        s_o = C.sem("oS")
        s_oe = C.sem("oeS")
        s_d = C.sem("dS")
        s_acc = C.sem("accS")

        with ExitStack() as psc:
            P = Ctx(nc, psc)
            ps_att = [P.ps(f"pa{i}", [128, 512], F32) for i in range(2)]
            ps_out = [P.ps(f"po{i}", [128, 1024], F32) for i in range(2)]
            ps_S = [P.ps(f"pS{i}", [128, 512], F32) for i in range(2)]

            with nc.Block() as blk:
                @blk.sync
                def _(sy):
                    for c in range(NT):
                        r = c % 2
                        if c >= 2:
                            sy.wait_ge(s_o.h, c - 1)
                        sy.dma_start(
                            out=kfs[r].rearrange("p (k t) -> p k t", t=128),
                            in_=kf_d.rearrange("(k p) t -> p k t", p=128)
                            [:, :, 128 * c:128 * (c + 1)]
                            ).then_inc(s_kf[r].h, 16)
                        s_kf[r].v += 16
                        sy.dma_start(
                            out=qzs[r].rearrange("p (k t) -> p k t", t=128),
                            in_=qz_d.rearrange("(k p) t -> p k t", p=128)
                            [:, :, 128 * c:128 * (c + 1)]
                            ).then_inc(s_qz[r].h, 16)
                        s_qz[r].v += 16
                        if c >= 2:
                            sy.wait_ge(s_d.h, c - 1)
                        sy.dma_start(out=kts[r],
                                     in_=ktm_d[128 * c:128 * (c + 1), :]
                                     ).then_inc(s_kt[r].h, 16)
                        s_kt[r].v += 16
                        sy.dma_start(out=vts[r],
                                     in_=vtm_d[128 * c:128 * (c + 1), :]
                                     ).then_inc(s_vt[r].h, 16)
                        s_vt[r].v += 16

                @blk.tensor
                def _(t):
                    t.wait_ge(s_c.h, 112)
                    for c in range(NT):
                        r = c % 2
                        t.wait_ge(s_kf[r].h, 16 * (c // 2 + 1))
                        t.wait_ge(s_qz[r].h, 16 * (c // 2 + 1))
                        for half in range(2):
                            G = 2 * c + half
                            if G >= 1:
                                t.wait_ge(s_m.h, G)
                            for j in range(4):
                                for par in range(2):
                                    h = 8 * half + 2 * j + par
                                    ins = t.matmul(
                                        ps_att[par][:, 128 * j:128 * (j + 1)],
                                        kfs[r][64 * par:64 * par + 64,
                                               128 * (h // 2):128 * (h // 2) + 128],
                                        qzs[r][64 * par:64 * par + 64,
                                               128 * (h // 2):128 * (h // 2) + 128],
                                        start=True, stop=True,
                                        skip_group_check=True)
                            s_a.inc(ins)
                        t.wait_ge(s_vt[r].h, 16 * (c // 2 + 1))
                        t.wait_ge(s_sb.h, c + 1)
                        if c >= 2:
                            t.wait_ge(s_oe.h, c - 1)
                        for h in range(H):
                            t.wait_ge(s_m.h, 2 * c + h // 8 + 1)
                            ins = t.matmul(
                                ps_out[r][64 * (h % 2):64 * (h % 2) + 64,
                                          128 * (h // 2):128 * (h // 2) + 128],
                                vts[r][:, 64 * h:64 * (h + 1)],
                                attb[:, 128 * h:128 * (h + 1)],
                                start=True, stop=not (sparts & 2),
                                skip_group_check=True)
                            if sparts & 2:
                                ins = t.matmul(
                                    ps_out[r][64 * (h % 2):64 * (h % 2) + 64,
                                              128 * (h // 2):128 * (h // 2) + 128],
                                    sbf[c % 2][64 * (h % 2):64 * (h % 2) + 64,
                                               64 * (h // 2):64 * (h // 2) + 64],
                                    qzs[r][64 * (h % 2):64 * (h % 2) + 64,
                                           128 * (h // 2):128 * (h // 2) + 128],
                                    start=False, stop=True, skip_group_check=True)
                        s_o.inc(ins)
                        t.wait_ge(s_kt[r].h, 16 * (c // 2 + 1))
                        if c >= 2:
                            t.wait_ge(s_acc.h, c - 1)
                        if sparts & 4:
                            for h in range(H):
                                ins = t.matmul(
                                    ps_S[r][64 * (h % 2):64 * (h % 2) + 64,
                                            64 * (h // 2):64 * (h // 2) + 64],
                                    kts[r][:, 64 * h:64 * (h + 1)],
                                    vts[r][:, 64 * h:64 * (h + 1)],
                                    start=True, stop=True, skip_group_check=True)
                        else:
                            ins = t.matmul(ps_S[r], kts[r][:, 0:128],
                                           vts[r][:, 0:512],
                                           start=True, stop=True,
                                           skip_group_check=True)
                        s_d.inc(ins)

                @blk.vector
                def _(v):
                    for c in range(NT):
                        r = c % 2
                        for half in range(2):
                            G = 2 * c + half
                            v.wait_ge(s_a.h, G + 1)
                            for j in range(4):
                                for par in range(2):
                                    h = 8 * half + 2 * j + par
                                    ins = v.tensor_mul(
                                        attb[:, 128 * h:128 * (h + 1)],
                                        ps_att[par][:, 128 * j:128 * (j + 1)],
                                        maskT)
                            s_m.inc(ins)
                        v.wait_ge(s_d.h, c + 1)
                        v.wait_ge(s_sb.h, c + 1)
                        ins = v.tensor_add(Sf32, Sf32, ps_S[r])
                        s_acc.inc(ins)
                        v.wait_ge(s_acc.h, s_acc.v)
                        ins = v.tensor_copy(out=sbf[(c + 1) % 2], in_=Sf32)
                        s_sb.inc(ins)

                @blk.scalar
                def _(s):
                    for c in range(NT):
                        r = c % 2
                        s.wait_ge(s_o.h, c + 1)
                        for g in range(NF):
                            ins = s.activation(
                                out=stream[g][:, 128 * c:128 * (c + 1)],
                                in_=ps_out[r][:, 128 * g:128 * (g + 1)],
                                func=AF.Copy)
                        s_oe.inc(ins)
        nc.all_engine_barrier()

        if nph < 3:
            return
        s_x = [C.sem("xq0"), C.sem("xq1")]
        s_pw = C.sem("pw")
        s_r1 = C.sem("r1S")
        s_ro = [C.sem("ro0"), C.sem("ro1")]
        s_st2 = C.sem("st2")
        s_sd2 = C.sem("sd2")
        s_xn2 = C.sem("xn2S")
        s_tp2 = C.sem("tp2")
        s_te2 = C.sem("te2")
        s_xo = C.sem("xoS")
        chw = Chain(C, "chw")

        with ExitStack() as psc:
            P = Ctx(nc, psc)
            ps_w = [P.ps(f"pw{i}", [128, 512], F32) for i in range(4)]
            ps_tpB = [P.ps(f"ptb{i}", [128, 1024], BF16) for i in range(2)]

            with nc.Block() as blk:
                @blk.sync
                def _(sy):
                    for q in range(4):
                        if q >= 1:
                            sy.wait_ge(s_xn2.h, q)
                            sy.wait_ge(s_ro[(q - 1) % 2].h,
                                       64 * ((q - 1) // 2 + 1))
                        for j in range(4):
                            r0 = 512 * q + 128 * j
                            sy.dma_start(out=xq[:, 1024 * j:1024 * (j + 1)],
                                         in_=xs[r0:r0 + 128, :]
                                         ).then_inc(s_x[q % 2].h, 16)
                            s_x[q % 2].v += 16

                @blk.tensor
                def _(t):
                    t.wait_ge(s_w.h, 48)
                    gi = 0

                    def transposes(t, q):
                        t.wait_ge(s_xn2.h, q + 1)
                        for j in range(4):
                            i = 4 * q + j
                            if i >= 2:
                                t.wait_ge(s_te2.h, i - 1)
                            for k in range(NF):
                                ins = t.transpose(
                                    ps_tpB[i % 2][:, 128 * k:128 * (k + 1)],
                                    xn2q[:, 1024 * j + 128 * k:
                                         1024 * j + 128 * (k + 1)],
                                    ident)
                            s_tp2.inc(ins)

                    for q in range(4):
                        for i in range(4 * q, 4 * q + 4):
                            for n in range(2):
                                if gi >= 4:
                                    t.wait_ge(s_r1.h, gi - 3)
                                for k in range(NF):
                                    t.matmul(
                                        ps_w[gi % 4],
                                        stream[k][:, 128 * i:128 * (i + 1)],
                                        wo_b[:, 1024 * k + 512 * n:
                                             1024 * k + 512 * (n + 1)],
                                        start=(k == 0), stop=False)
                                ins = t.matmul(ps_w[gi % 4], onesr,
                                               bo_b[:, 512 * n:512 * (n + 1)],
                                               start=False, stop=True)
                                s_pw.inc(ins)
                                gi += 1
                        if q >= 1:
                            transposes(t, q - 1)
                    transposes(t, 3)

                @blk.vector
                def _(v):
                    for q in range(4):
                        v.wait_ge(s_x[q % 2].h, 64 * (q // 2 + 1))
                        for j in range(4):
                            for n in range(2):
                                gi = 2 * (4 * q + j) + n
                                v.wait_ge(s_pw.h, gi + 1)
                                ins = v.scalar_tensor_tensor(
                                    xq[:, 1024 * j + 512 * n:
                                       1024 * j + 512 * (n + 1)],
                                    ps_w[gi % 4], 1.0,
                                    xq[:, 1024 * j + 512 * n:
                                       1024 * j + 512 * (n + 1)],
                                    ALU.mult, ALU.add)
                                s_r1.inc(ins)
                        v.wait_ge(s_r1.h, s_r1.v)
                        for j in range(4):
                            ins = v.reduce_sum(stat[:, 4 * q + j:4 * q + j + 1],
                                               xq[:, 1024 * j:1024 * (j + 1)],
                                               axis=AXX)
                        chw.bar(v, ins)
                        for j in range(4):
                            ins = v.tensor_scalar(
                                xcq[:, 1024 * j:1024 * (j + 1)],
                                xq[:, 1024 * j:1024 * (j + 1)],
                                1024.0, stat[:, 4 * q + j:4 * q + j + 1],
                                ALU.mult, ALU.subtract)
                        chw.bar(v, ins)
                        for j in range(4):
                            ins = v.tensor_mul(xcq[:, 1024 * j:1024 * (j + 1)],
                                               xcq[:, 1024 * j:1024 * (j + 1)],
                                               xcq[:, 1024 * j:1024 * (j + 1)])
                        chw.bar(v, ins)
                        for j in range(4):
                            ins = v.reduce_sum(
                                stat[:, 16 + 4 * q + j:16 + 4 * q + j + 1],
                                xcq[:, 1024 * j:1024 * (j + 1)], axis=AXX)
                        s_st2.inc(ins)
                        v.wait_ge(s_st2.h, s_st2.v)
                        for j in range(4):
                            ins = v.tensor_scalar(
                                xcq[:, 1024 * j:1024 * (j + 1)],
                                xq[:, 1024 * j:1024 * (j + 1)],
                                1024.0, stat[:, 4 * q + j:4 * q + j + 1],
                                ALU.mult, ALU.subtract)
                        chw.bar(v, ins)
                        v.wait_ge(s_sd2.h, q + 1)
                        ins = v.reciprocal(stat[:, 48 + 4 * q:48 + 4 * q + 4],
                                           stat[:, 32 + 4 * q:32 + 4 * q + 4])
                        chw.bar(v, ins)
                        if q >= 1:
                            v.wait_ge(s_tp2.h, 4 * q)
                            v.wait_ge(s_xo.h, 64 * q)
                        for j in range(4):
                            ins = v.tensor_scalar(
                                xn2q[:, 1024 * j:1024 * (j + 1)],
                                xcq[:, 1024 * j:1024 * (j + 1)],
                                stat[:, 48 + 4 * q + j:48 + 4 * q + j + 1],
                                1.0 / 1024.0, ALU.mult, ALU.mult)
                        s_xn2.inc(ins)

                @blk.scalar
                def _(s):
                    def evacs(s, q):
                        for j in range(4):
                            i = 4 * q + j
                            s.wait_ge(s_tp2.h, i + 1)
                            for k in range(NF):
                                ins = s.activation(
                                    out=xn2_fm[k][:, 128 * i:128 * (i + 1)],
                                    in_=ps_tpB[i % 2][:, 128 * k:128 * (k + 1)],
                                    func=AF.Copy)
                            s_te2.inc(ins)
                    for q in range(4):
                        s.wait_ge(s_st2.h, q + 1)
                        if q == 0:
                            s.wait_ge(s_cns.h, 1)
                        s_sd2.inc(s.activation(
                            out=stat[:, 32 + 4 * q:32 + 4 * q + 4],
                            in_=stat[:, 16 + 4 * q:16 + 4 * q + 4],
                            func=AF.Sqrt, bias=cns[:, 0:1], scale=cns[:, 1:2]))
                        if q >= 1:
                            evacs(s, q - 1)
                    evacs(s, 3)

                @blk.gpsimd
                def _(g):
                    for q in range(4):
                        for j in range(4):
                            i = 4 * q + j
                            g.wait_ge(s_r1.h, 2 * (i + 1))
                            g.dma_start(out=r1_o[128 * i:128 * (i + 1), :],
                                        in_=xq[:, 1024 * j:1024 * (j + 1)]
                                        ).then_inc(s_ro[q % 2].h, 16)
                            s_ro[q % 2].v += 16
                        g.wait_ge(s_xn2.h, q + 1)
                        for j in range(4):
                            i = 4 * q + j
                            g.dma_start(out=xn2_o[128 * i:128 * (i + 1), :],
                                        in_=xn2q[:, 1024 * j:1024 * (j + 1)]
                                        ).then_inc(s_xo.h, 16)
                            s_xo.v += 16
                    g.wait_ge(s_xo.h, s_xo.v)
                    g.wait_ge(s_ro[0].h, s_ro[0].v)
                    g.wait_ge(s_ro[1].h, s_ro[1].v)
        nc.all_engine_barrier()

        if nph < 4:
            return
        s_g1 = C.sem("g1")
        s_g2 = C.sem("g2")
        s_g3 = C.sem("g3")
        s_g4 = C.sem("g4")
        s_g5 = C.sem("g5")
        s_po = C.sem("poS")
        s_pd = C.sem("pdS")

        with ExitStack() as psc:
            P = Ctx(nc, psc)
            ps_ga = P.ps("psga", [128, 512], F32)
            ps_gb = P.ps("psgb", [128, 512], F32)
            ps_gc = P.ps("psgc", [128, 512], F32)

            with nc.Block() as blk:
                @blk.tensor
                def _(t):
                    for n in range(4):
                        if n >= 1:
                            t.wait_ge(s_g2.h, n)
                        for k in range(NF):
                            ins = t.matmul(
                                ps_ga[0:NE, :],
                                gw_b[:, NE * k:NE * (k + 1)],
                                xn2_fm[k][:, 512 * n:512 * (n + 1)],
                                start=(k == 0), stop=(k == NF - 1))
                        s_g1.inc(ins)
                    for n in range(4):
                        t.wait_ge(s_g2.h, n + 1)
                        if n >= 1:
                            t.wait_ge(s_g3.h, n)
                        ins = t.matmul(ps_gb[0:1, :], ones8,
                                       esc[0:NE, 512 * n:512 * (n + 1)],
                                       start=True, stop=True,
                                       skip_group_check=True)
                        s_g4.inc(ins)
                    for n in range(4):
                        t.wait_ge(s_g3.h, n + 1)
                        if n >= 1:
                            t.wait_ge(s_g5.h, n)
                        ins = t.matmul(ps_gc[0:NE, :], ones18,
                                       rec[0:1, 512 * n:512 * (n + 1)],
                                       start=True, stop=True,
                                       skip_group_check=True)
                        s_po.inc(ins)

                @blk.scalar
                def _(s):
                    for n in range(4):
                        s.wait_ge(s_g1.h, n + 1)
                        s_g2.inc(s.activation(out=esc[0:NE, 512 * n:512 * (n + 1)],
                                              in_=ps_ga[0:NE, :], func=AF.Exp))

                @blk.vector
                def _(v):
                    for n in range(4):
                        v.wait_ge(s_g4.h, n + 1)
                        ins = v.reciprocal(rec[0:1, 512 * n:512 * (n + 1)],
                                           ps_gb[0:1, :])
                        s_g3.inc(ins)
                        v.wait_ge(s_g3.h, s_g3.v)
                    for n in range(4):
                        v.wait_ge(s_po.h, n + 1)
                        ins = v.tensor_mul(prb[0:NE, 512 * n:512 * (n + 1)],
                                           esc[0:NE, 512 * n:512 * (n + 1)],
                                           ps_gc[0:NE, :])
                        s_g5.inc(ins)

                @blk.gpsimd
                def _(g):
                    g.wait_ge(s_g5.h, 4)
                    g.dma_start(out=probs_o, in_=prb[0:NE, :]).then_inc(s_pd.h, 16)
                    s_pd.v += 16
                    g.wait_ge(s_pd.h, s_pd.v)
        nc.all_engine_barrier()


def build_L2(nph=4, sparts=7):
    nc = bass.Bass("TRN2", target_bir_lowering=False, debug=False)
    xs = nc.dram_tensor("xs", [SL, D], F32, kind="ExternalInput").ap()
    qz_d = nc.dram_tensor("qz", [D, SL], BF16, kind="ExternalInput").ap()
    kf_d = nc.dram_tensor("kf", [D, SL], BF16, kind="ExternalInput").ap()
    ktm_d = nc.dram_tensor("ktm", [SL, D], BF16, kind="ExternalInput").ap()
    vtm_d = nc.dram_tensor("vtm", [SL, D], BF16, kind="ExternalInput").ap()
    sall_d = nc.dram_tensor("sall", [NCORE, 128, 512], F32, kind="ExternalInput").ap()
    msk_d = nc.dram_tensor("msk", [128, NCORE], F32, kind="ExternalInput").ap()
    wo_d = nc.dram_tensor("wo", [D, D], F32, kind="ExternalInput").ap()
    bo_d = nc.dram_tensor("bo", [1, D], F32, kind="ExternalInput").ap()
    gw_d = nc.dram_tensor("gw", [D, NE], F32, kind="ExternalInput").ap()
    r1_o = nc.dram_tensor("r1", [SL, D], F32, kind="ExternalOutput").ap()
    xn2_o = nc.dram_tensor("xn2", [SL, D], BF16, kind="ExternalOutput").ap()
    probs_o = nc.dram_tensor("probs", [NE, SL], F32, kind="ExternalOutput").ap()
    ident_np, maskT_np, _, _ = consts_np()
    ident_c = nc.inline_tensor(ident_np, name="identC").ap()
    maskT_c = nc.inline_tensor(maskT_np, name="maskTC").ap()
    ones_c = nc.inline_tensor(np.ones((1, 128), dtype=BFNP), name="onesC").ap()
    ones8f_c = nc.inline_tensor(np.ones((8, 1), dtype=np.float32), name="o8C").ap()
    ones18f_c = nc.inline_tensor(np.ones((1, 8), dtype=np.float32), name="o18C").ap()
    emit_L2(nc, xs, qz_d, kf_d, ktm_d, vtm_d, sall_d, msk_d, wo_d, bo_d, gw_d,
            r1_o, xn2_o, probs_o, ident_c, maskT_c, ones_c, ones8f_c,
            ones18f_c, NSRC=NCORE, nph=nph, sparts=sparts)
    return nc



def build_L3():
    nc = bass.Bass("TRN2", target_bir_lowering=False, debug=False)
    xn2f_d = nc.dram_tensor("xn2f", [T, D], BF16, kind="ExternalInput").ap()
    prow_d = nc.dram_tensor("prow", [1, T], F32, kind="ExternalInput").ap()
    w1_d = nc.dram_tensor("w1", [D, HID], F32, kind="ExternalInput").ap()
    w2_d = nc.dram_tensor("w2", [HID, D], F32, kind="ExternalInput").ap()

    Y_o = nc.dram_tensor("Y", [CAP, D], F32, kind="ExternalOutput").ap()
    slot_o = nc.dram_tensor("slot", [1, T], F32, kind="ExternalOutput").ap()

    ident_np, _, _, _ = consts_np()
    ident_c = nc.inline_tensor(ident_np, name="identC").ap()
    lt64_c = nc.inline_tensor(
        np.triu(np.ones((64, 64), np.float32), 1), name="lt64C").ap()
    o64sq_c = nc.inline_tensor(np.ones((64, 64), np.float32), name="o64C").ap()
    o64_c = nc.inline_tensor(np.ones((64, 1), np.float32), name="o641C").ap()

    st_dram = nc.dram_tensor("stD", [64, 128], F32, kind="Internal").ap()
    w2bf = nc.dram_tensor("w2bf", [HID, D], BF16, kind="Internal").ap()

    NITER = 40

    with ExitStack() as es:
        C = Ctx(nc, es)
        w1_b = C.sb("w1b", [128, NF * HID], BF16)
        w2r = [C.sb(f"w2r{i}", [128, 1024], BF16) for i in range(2)]
        xin = [C.sb(f"xin{i}", [128, 1024], BF16) for i in range(8)]
        xinT = [C.sb(f"xit{k}", [128, 1024], BF16) for k in range(NF)]
        hfm = [C.sb(f"hf{m}", [128, 1024], BF16) for m in range(32)]
        ybuf = [C.sb(f"yb{i}", [128, 1024], F32) for i in range(2)]
        pr = C.sb("prS", [64, 128], F32)
        th = C.sb("thS", [64, 1], F32)
        thm = C.sb("thmS", [64, 1], F32)
        bcol = C.sb("bcolS", [64, 1], F32)
        c64 = C.sb("c64S", [64, 1], F32)
        cnt64 = C.sb("cntS", [64, 1], F32)
        cmp64 = C.sb("cmp64", [64, 128], F32)
        m64 = C.sb("m64", [64, 128], F32)
        s64 = C.sb("s64", [64, 128], F32)
        z64 = C.sb("z64", [64, 128], F32)
        off64 = C.sb("off64", [64, 1], F32)
        sofA = C.sb("sofA", [64, 128], F32)
        indA = C.sb("indA", [64, 2048], F32)
        srowA = C.sb("srowA", [128, 2048], F32)
        iotaR = C.sb("iotaR", [64, 128], F32)
        iotaC = C.sb("iotaC", [128, 1], F32)
        sek = C.sb("sek", [64, 16], F32)
        nbf = C.sb("nbf", [128, 16], F32)
        nbi = C.sb("nbi", [128, 16], I32)
        scol = C.sb("scol", [128, 16], F32)
        wcmp = C.sb("wcmp", [128, 128], F32)
        wcol = C.sb("wcol", [128, 16], F32)
        idxf = C.sb("idxf", [128, 16], F32)
        idxi = C.sb("idxi", [128, 16], I32)
        lt64 = C.sb("lt64S", [64, 64], F32)
        o64sq = C.sb("o64sqS", [64, 64], F32)
        o64 = C.sb("o64S", [64, 1], F32)
        ident = C.sb("identS", [128, 128], BF16)

        s_c = C.sem("consts")
        s_w1 = C.sem("w1S")
        s_io = C.sem("ioS")
        s_w2p = [C.sem("w2p0"), C.sem("w2p1")]
        s_w2s = [C.sem("w2s0"), C.sem("w2s1")]
        chv = Chain(C, "chvL3")

        with nc.Block() as blk:
            @blk.sync
            def _(sy):
                sy.dma_start(out=ident, in_=ident_c).then_inc(s_c.h, 16)
                sy.dma_start(out=lt64, in_=lt64_c).then_inc(s_c.h, 16)
                sy.dma_start(out=o64sq, in_=o64sq_c).then_inc(s_c.h, 16)
                sy.dma_start(out=o64, in_=o64_c).then_inc(s_c.h, 16)
                sy.dma_start(out=pr,
                             in_=prow_d[0].rearrange("(k j) -> k j", j=128)
                             ).then_inc(s_c.h, 16)
                s_c.v += 80

            @blk.gpsimd
            def _(g):
                g.dma_start(out=w1_b.rearrange("p (k m) -> p k m", m=HID),
                            in_=w1_d.rearrange("(k p) m -> p k m", p=128)
                            ).then_inc(s_w1.h, 16)
                s_w1.v += 16
                g.iota(iotaR, [[1, 128]], channel_multiplier=0,
                       allow_small_or_imprecise_dtypes=True)
                g.iota(iotaC, [[0, 1]], channel_multiplier=1,
                       allow_small_or_imprecise_dtypes=True)
                g.memset(z64, 0.0)
                g.memset(th, 0.5).then_inc(s_io.h, 1)
                s_io.v += 1
                for kc in range(HID // 128):
                    r = kc % 2
                    if kc >= 2:
                        g.wait_ge(s_w2s[r].h, 16 * (kc // 2))
                    g.dma_start(out=w2r[r],
                                in_=w2_d[128 * kc:128 * (kc + 1), :]
                                ).then_inc(s_w2p[r].h, 16)
                    s_w2p[r].v += 16
                    g.wait_ge(s_w2p[r].h, s_w2p[r].v)
                    g.dma_start(out=w2bf[128 * kc:128 * (kc + 1), :],
                                in_=w2r[r]).then_inc(s_w2s[r].h, 16)
                    s_w2s[r].v += 16
                g.wait_ge(s_w2s[0].h, s_w2s[0].v)
                g.wait_ge(s_w2s[1].h, s_w2s[1].v)
                g.wait_ge(s_w1.h, s_w1.v)
        nc.all_engine_barrier()

        with ExitStack() as psc:
            P = Ctx(nc, psc)
            s_cd = P.sem("cdS")
            s_bi = P.sem("biS")
            ps_c = P.ps("pcS", [64, 1], F32)

            with nc.Block() as blk:
                @blk.vector
                def _(v):
                    v.wait_ge(s_io.h, 1)
                    v.wait_ge(s_c.h, 80)
                    for it in range(NITER):
                        step = 0.25 * (0.5 ** it)
                        ins = v.tensor_scalar(cmp64, pr, th[:, 0:1], None,
                                              ALU.is_gt)
                        chv.bar(v, ins)
                        ins = v.reduce_sum(c64, cmp64, axis=AXX)
                        s_cd.inc(ins)
                        v.wait_ge(s_bi.h, it + 1)
                        ins = v.tensor_copy(out=cnt64, in_=ps_c)
                        chv.bar(v, ins)
                        ins = v.tensor_scalar(bcol, cnt64, float(CAP), None,
                                              ALU.is_gt)
                        chv.bar(v, ins)
                        ins = v.tensor_scalar(thm, th, step, None,
                                              ALU.subtract)
                        chv.bar(v, ins)
                        ins = v.scalar_tensor_tensor(th, bcol, 2.0 * step, thm,
                                                     ALU.mult, ALU.add)
                        chv.bar(v, ins)

                @blk.tensor
                def _(t):
                    for it in range(NITER):
                        t.wait_ge(s_cd.h, it + 1)
                        s_bi.inc(t.matmul(ps_c, o64sq, c64,
                                          start=True, stop=True,
                                          skip_group_check=True))
        nc.all_engine_barrier()

        with ExitStack() as psc:
            P = Ctx(nc, psc)
            s_sv = P.sem("svS")
            s_so = P.sem("soS")
            s_l2 = P.sem("l2S")
            s_sg = P.sem("sgS")
            s_od = P.sem("odS")
            s_stw = P.sem("stwS")
            ps_off = P.ps("poffS", [64, 1], F32)
            ps_l1 = P.ps("pl1S", [128, 16], F32)

            with nc.Block() as blk:
                @blk.vector
                def _(v):
                    ins = v.tensor_scalar(m64, pr, th[:, 0:1], None, ALU.is_gt)
                    chv.bar(v, ins)
                    ins = v.tensor_tensor_scan(s64, m64, z64, 0.0,
                                               ALU.add, ALU.add)
                    s_sv.inc(ins)
                    v.wait_ge(s_so.h, 1)
                    ins = v.tensor_copy(out=off64, in_=ps_off)
                    chv.bar(v, ins)
                    ins = v.tensor_scalar(s64, s64, off64[:, 0:1], None,
                                          ALU.add)
                    s_sv.inc(ins)
                    v.wait_ge(s_sv.h, s_sv.v)
                    ins = v.tensor_scalar(sofA, s64, 1.0 + BIGIDX, None,
                                          ALU.subtract)
                    chv.bar(v, ins)
                    ins = v.tensor_mul(sofA, sofA, m64)
                    chv.bar(v, ins)
                    ins = v.tensor_scalar(sofA, sofA, BIGIDX, None, ALU.add)
                    s_sv.inc(ins)
                    for k in range(16):
                        ins = v.tensor_scalar(sek[:, k:k + 1], s64[:, 127:128],
                                              float(128 * k), None,
                                              ALU.subtract)
                        chv.bar(v, ins)
                        ins = v.tensor_scalar(
                            indA[:, 128 * k:128 * (k + 1)], iotaR,
                            sek[:, k:k + 1], None, ALU.is_ge)
                        if k < 15:
                            chv.bar(v, ins)
                    s_sv.inc(ins)
                    v.wait_ge(s_l2.h, 1)
                    ins = v.tensor_copy(out=nbf, in_=ps_l1)
                    chv.bar(v, ins)
                    ins = v.tensor_copy(out=nbi, in_=nbf)
                    s_sv.inc(ins)
                    v.wait_ge(s_sg.h, 16 * 16)
                    for k in range(16):
                        ins = v.tensor_scalar(scol[:, k:k + 1], iotaC,
                                              float(128 * k), None, ALU.add)
                        chv.bar(v, ins)
                        ins = v.tensor_scalar(
                            wcmp, srowA[:, 128 * k:128 * (k + 1)],
                            scol[:, k:k + 1], None, ALU.is_le)
                        chv.bar(v, ins)
                        ins = v.reduce_sum(wcol[:, k:k + 1], wcmp, axis=AXX)
                        chv.bar(v, ins)
                        ins = v.scalar_tensor_tensor(
                            idxf[:, k:k + 1], nbf[:, k:k + 1], 128.0,
                            wcol[:, k:k + 1], ALU.mult, ALU.add)
                        chv.bar(v, ins)
                    ins = v.tensor_copy(out=idxi, in_=idxf)
                    s_sv.inc(ins)

                @blk.tensor
                def _(t):
                    t.wait_ge(s_sv.h, 1)
                    s_so.inc(t.matmul(ps_off, lt64, s64[:, 127:128],
                                      start=True, stop=True,
                                      skip_group_check=True))
                    t.wait_ge(s_sv.h, 4)
                    for k in range(16):
                        ins = t.matmul(ps_l1[:, k:k + 1],
                                       indA[:, 128 * k:128 * (k + 1)],
                                       o64, start=True, stop=True,
                                       skip_group_check=True)
                    s_l2.inc(ins)

                @blk.sync
                def _(sy):
                    sy.wait_ge(s_sv.h, 2)
                    sy.dma_start(out=st_dram, in_=s64).then_inc(s_stw.h, 16)
                    s_stw.v += 16
                    sy.wait_ge(s_sv.h, 3)
                    sy.dma_start(out=slot_o[0].rearrange("(k j) -> k j", j=128),
                                 in_=sofA).then_inc(s_od.h, 16)
                    s_od.v += 16
                    sy.wait_ge(s_od.h, s_od.v)
                    sy.wait_ge(s_stw.h, s_stw.v)

                @blk.gpsimd
                def _(g):
                    g.wait_ge(s_sv.h, 5)
                    g.wait_ge(s_stw.h, 16)
                    for k in range(16):
                        g.indirect_dma_start(
                            srowA[:, 128 * k:128 * (k + 1)], None,
                            st_dram, IndirectOffsetOnAxis(nbi[:, k:k + 1], 0),
                            bounds_check=63, oob_is_err=False,
                        ).then_inc(s_sg.h, 16)
                        s_sg.v += 16
                    g.wait_ge(s_sg.h, s_sg.v)
        nc.all_engine_barrier()

        for hs in range(2):
            with ExitStack() as psc:
                P = Ctx(nc, psc)
                s_xg = [P.sem(f"xg{hs}{j}") for j in range(8)]
                s_tp = P.sem(f"tp{hs}")
                s_te = P.sem(f"te{hs}")
                s_f = P.sem(f"f{hs}")
                s_fe = P.sem(f"fe{hs}")
                ps_tp = P.ps(f"ptp{hs}", [128, 1024], BF16)
                ps_f = [P.ps(f"pf{hs}{i}", [128, 512], F32) for i in range(3)]

                with nc.Block() as blk:
                    @blk.gpsimd
                    def _(g):
                        for j in range(8):
                            g.indirect_dma_start(
                                xin[j], None, xn2f_d,
                                IndirectOffsetOnAxis(
                                    idxi[:, 8 * hs + j:8 * hs + j + 1], 0),
                                bounds_check=T - 1, oob_is_err=False,
                            ).then_inc(s_xg[j].h, 16)
                            s_xg[j].v += 16

                    @blk.tensor
                    def _(t):
                        for j in range(8):
                            t.wait_ge(s_xg[j].h, 16)
                            for k in range(NF):
                                ins = t.transpose(
                                    ps_tp[:, 128 * k:128 * (k + 1)],
                                    xin[j][:, 128 * k:128 * (k + 1)], ident)
                            s_tp.inc(ins)
                            t.wait_ge(s_te.h, j + 1)
                        gi = 0
                        for m in range(32):
                            for n in range(2):
                                if gi >= 3:
                                    t.wait_ge(s_fe.h, gi - 2)
                                for k in range(NF):
                                    ins = t.matmul(
                                        ps_f[gi % 3],
                                        w1_b[:, HID * k + 128 * m:
                                             HID * k + 128 * (m + 1)],
                                        xinT[k][:, 512 * n:512 * (n + 1)],
                                        start=(k == 0), stop=(k == NF - 1))
                                s_f.inc(ins)
                                gi += 1

                    @blk.scalar
                    def _(s):
                        for j in range(8):
                            s.wait_ge(s_tp.h, j + 1)
                            for k in range(NF):
                                ins = s.activation(
                                    out=xinT[k][:, 128 * j:128 * (j + 1)],
                                    in_=ps_tp[:, 128 * k:128 * (k + 1)],
                                    func=AF.Copy)
                            s_te.inc(ins)
                        gi = 0
                        for m in range(32):
                            for n in range(2):
                                s.wait_ge(s_f.h, gi + 1)
                                s_fe.inc(s.activation(
                                    out=hfm[m][:, 512 * n:512 * (n + 1)],
                                    in_=ps_f[gi % 3], func=AF.Gelu))
                                gi += 1
            nc.all_engine_barrier()

            for ph in range(2):
                with ExitStack() as psc:
                    P = Ctx(nc, psc)
                    s_wl = [P.sem(f"wl{hs}{ph}0"), P.sem(f"wl{hs}{ph}1")]
                    s_y = P.sem(f"y{hs}{ph}")
                    s_ye = P.sem(f"ye{hs}{ph}")
                    s_yo = [P.sem(f"yo{hs}{ph}0"), P.sem(f"yo{hs}{ph}1")]
                    ps_y = [P.ps(f"py{hs}{ph}{i}", [128, 1024], F32)
                            for i in range(4)]

                    with nc.Block() as blk:
                        @blk.sync
                        def _(sy):
                            for kc in range(32):
                                r = kc % 2
                                if kc >= 2:
                                    sy.wait_ge(s_y.h, kc - 1)
                                sy.dma_start(
                                    out=w2r[r],
                                    in_=w2bf[128 * kc:128 * (kc + 1), :]
                                    ).then_inc(s_wl[r].h, 16)
                                s_wl[r].v += 16

                        @blk.tensor
                        def _(t):
                            for kc in range(32):
                                r = kc % 2
                                t.wait_ge(s_wl[r].h, 16 * (kc // 2 + 1))
                                for mt in range(4):
                                    for n in range(2):
                                        ins = t.matmul(
                                            ps_y[mt][:, 512 * n:512 * (n + 1)],
                                            hfm[kc][:, 128 * (4 * ph + mt):
                                                    128 * (4 * ph + mt + 1)],
                                            w2r[r][:, 512 * n:512 * (n + 1)],
                                            start=(kc == 0), stop=(kc == 31))
                                s_y.inc(ins)

                        @blk.vector
                        def _(v):
                            v.wait_ge(s_y.h, 32)
                            for mt in range(4):
                                if mt >= 2:
                                    v.wait_ge(s_yo[mt % 2].h, 16)
                                ins = v.tensor_copy(out=ybuf[mt % 2],
                                                    in_=ps_y[mt])
                                s_ye.inc(ins)

                        @blk.gpsimd
                        def _(g):
                            for mt in range(4):
                                g.wait_ge(s_ye.h, mt + 1)
                                i8 = 8 * hs + 4 * ph + mt
                                g.dma_start(
                                    out=Y_o[128 * i8:128 * (i8 + 1), :],
                                    in_=ybuf[mt % 2]).then_inc(
                                        s_yo[mt % 2].h, 16)
                                s_yo[mt % 2].v += 16
                            g.wait_ge(s_yo[0].h, s_yo[0].v)
                            g.wait_ge(s_yo[1].h, s_yo[1].v)
                nc.all_engine_barrier()
    return nc



def build_L4():
    nc = bass.Bass("TRN2", target_bir_lowering=False, debug=False)
    acc_d = nc.dram_tensor("acc", [SL, D], F32, kind="ExternalInput").ap()
    Y_d = [nc.dram_tensor(f"Ye{e}", [CAP, D], F32, kind="ExternalInput").ap()
           for e in range(2)]
    slot_d = nc.dram_tensor("slotp", [2, SL], F32, kind="ExternalInput").ap()
    prb_d = nc.dram_tensor("prb2", [2, SL], F32, kind="ExternalInput").ap()

    out_o = nc.dram_tensor("accout", [SL, D], F32, kind="ExternalOutput").ap()

    with ExitStack() as es:
        C = Ctx(nc, es)
        acc = [C.sb(f"ac{i}", [128, D], F32) for i in range(NT)]
        scr = [C.sb(f"scr{i}", [128, D], F32) for i in range(2)]
        slt = C.sb("slt", [128, 2 * NT], F32)
        slti = C.sb("slti", [128, 2 * NT], I32)
        pmv = C.sb("pmv", [128, 2 * NT], F32)
        msk = C.sb("mskL", [128, 2 * NT], F32)

        s_al = [C.sem(f"al{i}") for i in range(NT)]
        s_sl = C.sem("slS")
        s_pm = C.sem("pmS")
        s_g = [C.sem(f"gS{n}") for n in range(2 * NT)]
        s_ms = C.sem("msS")
        s_ac = C.sem("acS")
        s_oo = [C.sem("oo0"), C.sem("oo1")]
        chv = Chain(C, "chvL4")

        with nc.Block() as blk:
            @blk.sync
            def _(sy):
                for i in range(NT):
                    sy.dma_start(out=acc[i], in_=acc_d[128 * i:128 * (i + 1), :]
                                 ).then_inc(s_al[i].h, 16)
                    s_al[i].v += 16
                for e in range(2):
                    for i in range(NT):
                        sy.dma_start(out=slt[:, NT * e + i:NT * e + i + 1],
                                     in_=slot_d[e, 128 * i:128 * (i + 1)]
                                     .rearrange("(p o) -> p o", o=1)
                                     ).then_inc(s_sl.h, 16)
                        s_sl.v += 16
                        sy.dma_start(out=pmv[:, NT * e + i:NT * e + i + 1],
                                     in_=prb_d[e, 128 * i:128 * (i + 1)]
                                     .rearrange("(p o) -> p o", o=1)
                                     ).then_inc(s_sl.h, 16)
                        s_sl.v += 16

            @blk.vector
            def _(v):
                v.wait_ge(s_sl.h, s_sl.v)
                ins = v.tensor_copy(out=slti, in_=slt)
                chv.bar(v, ins)
                ins = v.tensor_scalar(msk, slt, float(CAP), None, ALU.is_lt)
                chv.bar(v, ins)
                ins = v.tensor_mul(pmv, pmv, msk)
                s_pm.inc(ins)
                n = 0
                for e in range(2):
                    for i in range(NT):
                        v.wait_ge(s_g[n].h, 16)
                        if e == 0:
                            v.wait_ge(s_al[i].h, 16)
                        col = NT * e + i
                        ins = v.scalar_tensor_tensor(
                            acc[i], scr[n % 2], pmv[:, col:col + 1],
                            acc[i], ALU.mult, ALU.add)
                        s_ac.inc(ins)
                        v.wait_ge(s_ac.h, s_ac.v)
                        n += 1

            @blk.gpsimd
            def _(g):
                g.memset(scr[0], 0.0)
                g.memset(scr[1], 0.0).then_inc(s_ms.h, 1)
                g.wait_ge(s_ms.h, 1)
                g.wait_ge(s_pm.h, 1)
                n = 0
                for e in range(2):
                    for i in range(NT):
                        if n >= 2:
                            g.wait_ge(s_ac.h, n - 1)
                        col = NT * e + i
                        g.indirect_dma_start(
                            scr[n % 2], None, Y_d[e],
                            IndirectOffsetOnAxis(slti[:, col:col + 1], 0),
                            bounds_check=CAP - 1, oob_is_err=False,
                        ).then_inc(s_g[n].h, 16)
                        s_g[n].v += 16
                        n += 1

            @blk.scalar
            def _(s):
                for i in range(NT):
                    s.wait_ge(s_ac.h, NT + i + 1)
                    s.dma_start(out=out_o[128 * i:128 * (i + 1), :],
                                in_=acc[i]).then_inc(s_oo[i % 2].h, 16)
                    s_oo[i % 2].v += 16
                s.wait_ge(s_oo[0].h, s_oo[0].v)
                s.wait_ge(s_oo[1].h, s_oo[1].v)
    return nc





ECAP = 1152
NST = ECAP // 128


def build_fused():
    nc = bass.Bass("TRN2", target_bir_lowering=False, debug=False)
    xq2 = nc.dram_tensor("xq2", [2, SL, D], F32, kind="ExternalInput").ap()
    wq_d = nc.dram_tensor("wq", [D, D], F32, kind="ExternalInput").ap()
    wk_d = nc.dram_tensor("wk", [D, D], F32, kind="ExternalInput").ap()
    wv_d = nc.dram_tensor("wv", [D, D], F32, kind="ExternalInput").ap()
    wo_d = nc.dram_tensor("wo", [D, D], F32, kind="ExternalInput").ap()
    bo_d = nc.dram_tensor("bo", [1, D], F32, kind="ExternalInput").ap()
    gw_d = nc.dram_tensor("gw", [D, NE], F32, kind="ExternalInput").ap()
    w1e_d = nc.dram_tensor("w1e", [4, D, HID], F32, kind="ExternalInput").ap()
    w2e_d = nc.dram_tensor("w2e", [4, HID, D], F32, kind="ExternalInput").ap()
    msks_d = nc.dram_tensor("msks", [2, 128, 4], F32, kind="ExternalInput").ap()
    meta_d = nc.dram_tensor("meta", [1, 8], I32, kind="ExternalInput").ap()

    out_o = nc.dram_tensor("out", [SL, D], F32, kind="ExternalOutput").ap()

    qzP = nc.dram_tensor("qzP", [2, D, SL], BF16, kind="Internal").ap()
    kfP = nc.dram_tensor("kfP", [2, D, SL], BF16, kind="Internal").ap()
    ktP = nc.dram_tensor("ktP", [2, SL, D], BF16, kind="Internal").ap()
    vtP = nc.dram_tensor("vtP", [2, SL, D], BF16, kind="Internal").ap()
    r1P = nc.dram_tensor("r1P", [SL, D], F32, kind="Internal").ap()
    stP = nc.dram_tensor("stP", [32, 128], F32, kind="Internal").ap()
    w2bfP = nc.dram_tensor("w2bfP", [4, HID, D], BF16, kind="Internal").ap()

    SH_S = nc.dram_tensor("SH_S", [4, 128, 512], F32, kind="Internal",
                          addr_space="Shared").ap()
    SH_PR = nc.dram_tensor("SH_PR", [4, NE, SL], F32, kind="Internal",
                           addr_space="Shared").ap()
    SH_XN = nc.dram_tensor("SH_XN", [2 * SL, D], BF16, kind="Internal",
                           addr_space="Shared").ap()
    SH_Y = nc.dram_tensor("SH_Y", [NE, ECAP, D], F32, kind="Internal",
                          addr_space="Shared").ap()
    SH_SL = nc.dram_tensor("SH_SL", [NE, 2 * SL], F32, kind="Internal",
                           addr_space="Shared").ap()
    SH_FL = nc.dram_tensor("SH_FL", [2, 16], I32, kind="Internal",
                           addr_space="Shared").ap()

    ident_np, maskT_np, blockpat_np, E_np = consts_np()
    ident_c = nc.inline_tensor(ident_np, name="identC").ap()
    maskT_c = nc.inline_tensor(maskT_np, name="maskTC").ap()
    bp_c = nc.inline_tensor(blockpat_np, name="bpC").ap()
    E_c = nc.inline_tensor(E_np, name="EC").ap()
    ones_c = nc.inline_tensor(np.ones((1, 128), dtype=BFNP), name="onesC").ap()
    ones8f_c = nc.inline_tensor(np.ones((8, 1), np.float32), name="o8C").ap()
    ones18f_c = nc.inline_tensor(np.ones((1, 8), np.float32), name="o18C").ap()
    lt32_c = nc.inline_tensor(
        np.triu(np.ones((32, 32), np.float32), 1), name="lt32C").ap()
    o32sq_c = nc.inline_tensor(np.ones((64, 64), np.float32), name="o64C").ap()
    o32_c = nc.inline_tensor(np.ones((32, 1), np.float32), name="o321C").ap()

    NITER = 40

    with ExitStack() as es:
        C = Ctx(nc, es)
        ident = C.sb("identS", [128, 128], BF16)
        maskT = C.sb("maskTS", [128, 128], BF16)
        bp = C.sb("bpS", [128, 16 * NF], BF16)
        Ehb = C.sb("EhbS", [16, D], BF16)
        onesr = C.sb("onesr", [1, 128], BF16)
        ones8 = C.sb("ones8S", [8, 1], F32)
        ones18 = C.sb("ones18S", [1, 8], F32)
        lt32 = C.sb("lt32S", [32, 32], F32)
        o64sq = C.sb("o64sqS", [64, 64], F32)
        o32 = C.sb("o32S", [32, 1], F32)
        cns = C.sb("cnsS", [128, 2], F32)
        msks = C.sb("msksS", [128, 8], F32)
        meta = C.sb("metaS", [1, 8], I32)
        flagw = C.sb("flagw", [1, 1], I32)
        flagr = C.sb("flagr", [1, 1], I32)
        iotaR = C.sb("iotaRS", [64, 128], F32)
        iotaC = C.sb("iotaCS", [128, 1], F32)

        s_c = C.sem("constsF")
        s_w = C.sem("wF")
        s_cns = C.sem("cnsF")
        s_flw = C.sem("flwF")
        s_flr = C.sem("flrF")

        def load_meta_reg(eng, idx):
            return eng.value_load(meta[0:1, idx:idx + 1])

        def pair_barrier(g, bar_idx, rrole):
            g.memset(flagw, bar_idx + 1)
            g.dma_start(out=SH_FL[bass.ds(rrole, 1), bar_idx:bar_idx + 1],
                        in_=flagw).then_inc(s_flw.h, 16)
            s_flw.v += 16
            g.wait_ge(s_flw.h, s_flw.v)
            with g.register(f"pc{bar_idx}") as rc, \
                 g.register(f"pv{bar_idx}") as rv:
                g.reg_mov(rc, s_flw.v)
                with nc.bb(f"poll{bar_idx}"):
                    g.dma_start(out=flagr,
                                in_=SH_FL.rearrange("a b -> (a b)")
                                [bass.ds(16 - 16 * rrole + bar_idx, 1)]
                                .rearrange("(a b) -> a b", a=1)
                                ).then_inc(s_flw.h, 16)
                    s_flw.v += 16
                    g.reg_add(rc, rc, 16)
                    g.wait_ge(s_flw.h, rc)
                    v = g.value_load(flagr)
                    g.br_lt(v, bar_idx + 1, f"poll{bar_idx}", f"pdone{bar_idx}")
                with nc.bb(f"pdone{bar_idx}"):
                    g.engine_nop()


def _stack(arrs):
    return np.ascontiguousarray(np.stack(arrs))


def kernel(L0_attn_x, L0_attn_Wq, L0_attn_Wk, L0_attn_Wv, L0_attn_Wo,
           L0_attn_bo, L0_ffn_gate_w, L0_ffn_w1, L0_ffn_w2):
    x = np.ascontiguousarray(np.asarray(L0_attn_x))
    Wq = np.ascontiguousarray(np.asarray(L0_attn_Wq))
    Wk = np.ascontiguousarray(np.asarray(L0_attn_Wk))
    Wv = np.ascontiguousarray(np.asarray(L0_attn_Wv))
    Wo = np.ascontiguousarray(np.asarray(L0_attn_Wo))
    bo = np.ascontiguousarray(np.asarray(L0_attn_bo)).reshape(1, D)
    gw = np.ascontiguousarray(np.asarray(L0_ffn_gate_w))
    w1 = np.ascontiguousarray(np.asarray(L0_ffn_w1))
    w2 = np.ascontiguousarray(np.asarray(L0_ffn_w2))
    cores = list(range(NCORE))

    def xsl(c):
        return np.ascontiguousarray(x[c // 4, SL * (c % 4):SL * (c % 4 + 1)])

    nc1 = build_L1()
    im1 = [{"xs": xsl(c), "wq": Wq, "wk": Wk, "wv": Wv} for c in cores]
    r1 = run_bass_kernel_spmd(nc1, im1, core_ids=cores).results

    sall = _stack([r1[c]["ssum"] for c in cores])
    nc2 = build_L2()
    im2 = []
    for c in cores:
        msk = np.zeros((128, NCORE), np.float32)
        msk[:, 4 * (c // 4):c] = 1.0
        im2.append({"xs": xsl(c), "qz": r1[c]["qz"], "kf": r1[c]["kf"],
                    "ktm": r1[c]["ktm"], "vtm": r1[c]["vtm"], "sall": sall,
                    "msk": msk, "wo": Wo, "bo": bo, "gw": gw})
    r2 = run_bass_kernel_spmd(nc2, im2, core_ids=cores).results

    xn2full = [np.ascontiguousarray(
        np.concatenate([r2[4 * b + j]["xn2"] for j in range(4)], axis=0))
        for b in range(B)]
    probsfull = [np.ascontiguousarray(
        np.concatenate([r2[4 * b + j]["probs"] for j in range(4)], axis=1))
        for b in range(B)]
    nc3 = build_L3()
    r3 = []
    for b in range(B):
        im3 = [{"xn2f": xn2full[b],
                "prow": np.ascontiguousarray(probsfull[b][c:c + 1]),
                "w1": np.ascontiguousarray(w1[c]),
                "w2": np.ascontiguousarray(w2[c])} for c in cores]
        r3.append(run_bass_kernel_spmd(nc3, im3, core_ids=cores).results)

    nc4 = build_L4()
    acc = [np.ascontiguousarray(r2[c]["r1"]) for c in cores]
    for l in range(4):
        im4 = []
        for c in cores:
            b = c // 4
            t0 = SL * (c % 4)
            slotp = _stack([r3[b][2 * l + e]["slot"][0][t0:t0 + SL]
                            for e in range(2)])
            prb2 = np.ascontiguousarray(r2[c]["probs"][2 * l:2 * l + 2])
            im4.append({"acc": acc[c], "Ye0": r3[b][2 * l]["Y"],
                        "Ye1": r3[b][2 * l + 1]["Y"],
                        "slotp": slotp, "prb2": prb2})
        r4 = run_bass_kernel_spmd(nc4, im4, core_ids=cores).results
        acc = [r4[c]["accout"] for c in cores]

    out = np.empty((B, T, D), np.float32)
    for c in cores:
        out[c // 4, SL * (c % 4):SL * (c % 4 + 1)] = acc[c]
    return out
